# version 02
# baseline (speedup 1.0000x reference)
import contextlib
import numpy as np
import concourse.bass as bass
import concourse.mybir as mybir
from concourse.bass_utils import run_bass_kernel_spmd

F32 = mybir.dt.float32
F16 = mybir.dt.float16
AF = mybir.ActivationFunctionType
ALU = mybir.AluOpType

D = 2048
NCH = 16
T = 512
NT = 4
HD = 8192
NH = 64
LCORE = 2048
FWHT_SCALE = float(1.0 / np.sqrt(HD))
EPS = 1e-5

ENGINES = ("sync", "tensor", "vector", "scalar", "gpsimd")
SEM_NAMES = ["t", "v", "sc", "cast", "ypf", "pp13", "pp2", "xr0", "xr1",
             "ya", "yb", "w1a", "w1b_", "w3a", "w3b_", "w2a", "w2b_",
             "xa", "xb", "iq"]


def build_program(debug=False):
    nc = bass.Bass("TRN2", target_bir_lowering=False)

    xfm = nc.dram_tensor("xfm", [D, LCORE], F32, kind="ExternalInput").ap()
    xpart = nc.dram_tensor("xpart", [D, LCORE], F32, kind="ExternalInput").ap()
    w1t = nc.dram_tensor("w1t", [D, HD], F32, kind="ExternalInput").ap()
    w3t = nc.dram_tensor("w3t", [D, HD], F32, kind="ExternalInput").ap()
    w2t = nc.dram_tensor("w2t", [HD, D], F32, kind="ExternalInput").ap()
    igw = nc.dram_tensor("igw", [16, 128, 128], F32, kind="ExternalInput").ap()
    agw = nc.dram_tensor("agw", [16, 128, 128], F32, kind="ExternalInput").ap()
    igb = nc.dram_tensor("igb", [128, 16], F32, kind="ExternalInput").ap()
    agb = nc.dram_tensor("agb", [128, 16], F32, kind="ExternalInput").ap()
    apar = nc.dram_tensor("apar", [128, 16], F32, kind="ExternalInput").ap()
    h128 = nc.dram_tensor("h128", [128, 128], F16, kind="ExternalInput").ap()
    ones128 = nc.dram_tensor("ones128", [128, 128], F16, kind="ExternalInput").ap()
    ones32d = nc.dram_tensor("ones32d", [128, 128], F32, kind="ExternalInput").ap()
    cst = nc.dram_tensor("cst", [128, 4], F32, kind="ExternalInput").ap()
    y = nc.dram_tensor("y", [D, LCORE], F32, kind="ExternalOutput").ap()
    if debug:
        dbg_s = nc.dram_tensor("dbg_s", [NT, 128, NCH * T], F16, kind="ExternalOutput").ap()
        dbg_y = nc.dram_tensor("dbg_y", [NT, NCH, 128, T], F32, kind="ExternalOutput").ap()
        dbg_xn = nc.dram_tensor("dbg_xn", [NT, 128, NCH * T], F16, kind="ExternalOutput").ap()
    w1s = nc.dram_tensor("w1s", [D, HD], F16, kind="Internal").ap()
    w3s = nc.dram_tensor("w3s", [D, HD], F16, kind="Internal").ap()
    w2s = nc.dram_tensor("w2s", [HD, D], F16, kind="Internal").ap()

    sb = {}

    def S(name, shape, dtype):
        sb[name] = nc.alloc_sbuf_tensor("sb_" + name, shape, dtype)

    S("cst", [128, 4], F32)
    S("h128", [128, 128], F16)
    S("ones", [128, 128], F16)
    S("ones32", [128, 128], F32)
    S("igw16", [128, NCH * 128], F16)
    S("agw16", [128, NCH * 128], F16)
    S("igb", [128, 16], F32)
    S("agb", [128, 16], F32)
    S("apar", [128, 16], F32)
    S("spm", [128, 16], F32)
    S("carry", [128, 16], F32)
    S("xn16p", [128, NCH * T], F16)
    S("s16p", [128, NCH * T], F16)
    S("mean32", [128, T], F32)
    S("rstd32", [128, T], F32)
    S("vv32", [128, T], F32)
    for p in range(2):
        S(f"w1b{p}", [128, NCH * 128], F16)
        S(f"w3b{p}", [128, NCH * 128], F16)
        S(f"w2b{p}", [128, 8 * 128], F16)
        S(f"sg{p}", [128, T], F16)
        S(f"hb{p}", [128, T], F16)
        S(f"g16{p}", [128, T], F16)
        S(f"y32{p}", [128, T], F32)
    o2 = nc.sbuf_base
    S("hd2", [128, NH * T], F16)
    o3 = nc.sbuf_base
    S("hd3", [128, NH * T], F16)
    sb["zx32"] = nc.alloc_sbuf_tensor_at("sb_zx32", [128, NCH * T], F32, offset=o2)
    sb["zx16"] = nc.alloc_sbuf_tensor_at("sb_zx16", [128, NCH * T], F16,
                                         offset=o2 + 32768)
    sb["zsq"] = nc.alloc_sbuf_tensor_at("sb_zsq", [128, NCH * T], F16,
                                        offset=o2 + 49152)
    sb["zgx"] = nc.alloc_sbuf_tensor_at("sb_zgx", [128, NCH * T], F16, offset=o3)
    sb["zga"] = nc.alloc_sbuf_tensor_at("sb_zga", [128, NCH * T], F16,
                                        offset=o3 + 16384)
    sb["za"] = nc.alloc_sbuf_tensor_at("sb_za", [128, NCH * T], F16,
                                       offset=o3 + 32768)
    sb["zmu"] = nc.alloc_sbuf_tensor_at("sb_zmu", [128, NCH * T], F16,
                                        offset=o3 + 49152)

    ps = {}
    for name in ("psA", "psB", "psC", "psD", "psE", "psF"):
        ps[name] = nc.alloc_psum_tensor("ps_" + name, [128, T], F32)

    ops = {e: [] for e in ENGINES}
    ctr = {s: 0 for s in SEM_NAMES}

    def inc(sem, amt):
        ctr[sem] += amt
        return ctr[sem]

    def wait(eng, sem, val):
        assert val <= ctr[sem], (eng, sem, val, ctr[sem])
        if val > 0:
            ops[eng].append(("wait", sem, val))

    def emit(eng, fn, sem=None, amt=1, waits=()):
        for (s, v) in waits:
            wait(eng, s, v)
        if sem is None:
            ops[eng].append(("op", fn, None, 0))
            return None
        v = inc(sem, amt)
        ops[eng].append(("op", fn, sem, amt))
        return v

    def dma(q, out_ap, in_ap, sem, waits=(), **kw):
        return emit(q, (lambda e, o=out_ap, i=in_ap, kw=kw:
                        e.dma_start(out=o, in_=i, **kw)), sem=sem, amt=16,
                    waits=waits)

    def barrier(tag, full=False):
        tv, vv, sv = ctr["t"], ctr["v"], ctr["sc"]
        engs = ("tensor", "vector", "scalar", "sync", "gpsimd") if full else                ("tensor", "vector", "scalar")
        sems = list(ctr.items()) if full else [("t", tv), ("v", vv), ("sc", sv)]
        for eng in engs:
            for s, v in sems:
                if v > 0:
                    ops[eng].append(("wait", s, v))

    def A(name):
        return sb[name].ap()

    def ch_sl(buf, c, width=T):
        return sb[buf].ap()[:, c * width:(c + 1) * width]

    dma("sync", A("cst")[:], cst, "iq")
    dma("sync", A("h128")[:], h128, "iq")
    dma("sync", A("ones")[:], ones128, "iq")
    dma("sync", A("ones32")[:], ones32d, "iq")
    dma("sync", A("igb")[:], igb, "iq")
    dma("sync", A("agb")[:], agb, "iq")
    dma("sync", A("apar")[:], apar, "iq")
    v_init_loads = ctr["iq"]
    igw_v = igw.rearrange("h i j -> i h j")
    agw_v = agw.rearrange("h i j -> i h j")
    igw16_v = A("igw16").rearrange("p (h j) -> p h j", h=16)
    agw16_v = A("agw16").rearrange("p (h j) -> p h j", h=16)
    dma("gpsimd", igw16_v, igw_v, "cast")
    dma("gpsimd", agw16_v, agw_v, "cast")
    v_cast = ctr["cast"]
    for j in range(16):
        dma("gpsimd", w1s[:, j * 512:(j + 1) * 512],
            w1t[:, j * 512:(j + 1) * 512], "pp13")
        dma("gpsimd", w3s[:, j * 512:(j + 1) * 512],
            w3t[:, j * 512:(j + 1) * 512], "pp13")
    v_pp13 = ctr["pp13"]
    for j in range(4):
        dma("gpsimd", w2s[:, j * 512:(j + 1) * 512],
            w2t[:, j * 512:(j + 1) * 512], "pp2")
    v_pp2 = ctr["pp2"]

    v0 = emit("scalar", lambda e: e.activation(out=A("spm")[:], in_=A("apar")[:],
                                               func=AF.Exp), sem="sc",
              waits=[("iq", v_init_loads)])
    v1 = emit("vector", lambda e: e.tensor_scalar_add(out=A("spm")[:],
                                                      in0=A("spm")[:],
                                                      scalar1=1.0), sem="v",
              waits=[("sc", v0)])
    v2 = emit("scalar", lambda e: e.activation(out=A("spm")[:], in_=A("spm")[:],
                                               func=AF.Ln), sem="sc",
              waits=[("v", v1)])
    v3 = emit("vector", lambda e: e.tensor_scalar_mul(out=A("spm")[:],
                                                      in0=A("spm")[:],
                                                      scalar1=-8.0), sem="v",
              waits=[("sc", v2)])
    v_carry = emit("vector", lambda e: e.memset(A("carry")[:], 0.0), sem="v")

    st = {
        "carry_v": v_carry,
        "dm_done_t": 0,
        "hd_evac_sc": 0,
        "gates_sc_prev": 0,
        "zx32_free": [],
        "silu_sc": {}, "hb_v": {}, "h128_t": {}, "sc_hev": {},
        "ug_t": {}, "ul_t": {}, "w1_consumed_t": {}, "w2_consumed_t": {},
        "g16_sc": {}, "y_v": {},
    }

    def phase1(pt):
        own = pt >= 4
        tile = pt % 4
        src = xfm if own else xpart
        xsem = ["xa", "xb"][pt % 2]
        src_v = src.rearrange("(c p) t -> p c t", p=128)[:, :, tile * T:(tile + 1) * T]
        dst_v = A("zx32").rearrange("p (c t) -> p c t", c=NCH)
        xw = [("t", st["dm_done_t"])] + st["zx32_free"]
        vx = dma("sync", dst_v, src_v, xsem, waits=xw)
        vc2 = emit("scalar", lambda e: e.activation(out=A("zsq")[:],
                                                    in_=A("zx32")[:],
                                                    func=AF.Square), sem="sc",
                   waits=[(xsem, vx)])
        for c in range(NCH):
            emit("tensor", (lambda e, c=c: e.matmul(
                ps["psA"].ap()[:], A("ones32")[:], ch_sl("zx32", c),
                start=(c == 0), stop=(c == NCH - 1))),
                sem="t" if c == NCH - 1 else None,
                waits=[(xsem, vx), ("t", st["dm_done_t"])] if c == 0 else ())
        v_ss = ctr["t"]
        for c in range(NCH):
            emit("tensor", (lambda e, c=c: e.matmul(
                ps["psB"].ap()[:], A("ones")[:], ch_sl("zsq", c),
                start=(c == 0), stop=(c == NCH - 1))),
                sem="t" if c == NCH - 1 else None,
                waits=[("sc", vc2)] if c == 0 else ())
        v_sq = ctr["t"]
        emit("vector", lambda e: e.tensor_scalar_mul(
            out=A("mean32")[:], in0=ps["psA"].ap()[:], scalar1=1.0 / D),
            sem="v", waits=[("t", v_ss)])
        emit("vector", lambda e: e.tensor_scalar_mul(
            out=A("vv32")[:], in0=ps["psB"].ap()[:], scalar1=1.0 / D),
            sem="v", waits=[("t", v_sq)])
        emit("vector", lambda e: e.tensor_mul(
            out=A("rstd32")[:], in0=A("mean32")[:], in1=A("mean32")[:]), sem="v")
        vr = emit("vector", lambda e: e.tensor_sub(
            out=A("vv32")[:], in0=A("vv32")[:], in1=A("rstd32")[:]), sem="v")
        vsq = emit("scalar", lambda e: e.activation(
            out=A("vv32")[:], in_=A("vv32")[:], func=AF.Ln,
            bias=A("cst")[:, 1:2]), sem="sc", waits=[("v", vr)])
        vrc2 = emit("scalar", lambda e: e.activation(
            out=A("rstd32")[:], in_=A("vv32")[:], func=AF.Exp,
            scale=-0.5), sem="sc")
        bank_ig = ["psA", "psC"]
        bank_ag = ["psB", "psD"]
        ev_ig, ev_ag = {}, {}
        v_xn_c = {}
        for c in range(NCH):
            emit("vector", (lambda e, c=c: e.tensor_sub(
                out=ch_sl("zx32", c), in0=ch_sl("zx32", c), in1=A("mean32")[:])),
                sem=None, waits=[("sc", vrc2)] if c == 0 else ())
            v_xn_c[c] = emit("vector", (lambda e, c=c: e.tensor_mul(
                out=ch_sl("xn16p", c), in0=ch_sl("zx32", c), in1=A("rstd32")[:])),
                sem="v")
        v_xn = ctr["v"]
        for c in range(NCH):
            big, bag = bank_ig[c % 2], bank_ag[c % 2]
            w_ = [("v", v_xn_c[c])]
            if c == 0:
                w_.append(("cast", v_cast))
            if c >= 2:
                w_.append(("sc", ev_ig[c - 2]))
            vt1 = emit("tensor", (lambda e, c=c, b=big: e.matmul(
                ps[b].ap()[:], A("igw16")[:, c * 128:(c + 1) * 128],
                ch_sl("xn16p", c), start=True, stop=True)), sem="t", waits=w_)
            w2_ = [("sc", ev_ag[c - 2])] if c >= 2 else []
            vt2 = emit("tensor", (lambda e, c=c, b=bag: e.matmul(
                ps[b].ap()[:], A("agw16")[:, c * 128:(c + 1) * 128],
                ch_sl("xn16p", c), start=True, stop=True)), sem="t", waits=w2_)
            ev_ig[c] = emit("scalar", (lambda e, c=c, b=big: e.activation(
                out=ch_sl("zgx", c), in_=ps[b].ap()[:], func=AF.Sigmoid,
                bias=A("igb")[:, c:c + 1])), sem="sc", waits=[("t", vt1)])
            ev_ag[c] = emit("scalar", (lambda e, c=c, b=bag: e.activation(
                out=ch_sl("zga", c), in_=ps[b].ap()[:], func=AF.Sigmoid,
                bias=A("agb")[:, c:c + 1])), sem="sc", waits=[("t", vt2)])
        v_gates_sc = ctr["sc"]
        for c in range(NCH):
            emit("vector", (lambda e, c=c: e.tensor_scalar_mul(
                out=ch_sl("zga", c), in0=ch_sl("zga", c),
                scalar1=A("spm")[:, c:c + 1])),
                sem="v" if c == NCH - 1 else None,
                waits=[("sc", v_gates_sc), ("v", v3)] if c == 0 else ())
        v_loga = ctr["v"]
        va = emit("scalar", lambda e: e.activation(
            out=A("za")[:], in_=A("zga")[:], func=AF.Exp), sem="sc",
            waits=[("v", v_loga)])
        emit("scalar", lambda e: e.activation(
            out=A("zsq")[:], in_=A("zga")[:], func=AF.Exp, scale=2.0), sem="sc")
        vmu = emit("scalar", lambda e: e.activation(
            out=A("zmu")[:], in_=A("zsq")[:], func=AF.Sqrt, scale=-1.0,
            bias=A("cst")[:, 0:1]), sem="sc")
        mu3 = A("zmu").rearrange("p (c t) -> p c t", c=NCH)[:, :, 0:1]
        if pt == 0:
            v_rst = emit("vector", lambda e: e.memset(mu3, 1.0), sem="v",
                         waits=[("sc", vmu)])
        elif pt == 4:
            v_rst = emit("vector", lambda e: e.tensor_scalar(
                out=mu3, in0=mu3, scalar1=A("cst")[:, 3:4],
                scalar2=A("cst")[:, 2:3], op0=ALU.mult, op1=ALU.add), sem="v",
                waits=[("sc", vmu)])
        else:
            v_rst = None
        emit("vector", lambda e: e.tensor_mul(
            out=A("zgx")[:], in0=A("xn16p")[:], in1=A("zgx")[:]), sem=None,
            waits=[("sc", v_gates_sc)])
        xi_w = [("v", v_rst)] if v_rst is not None else [("sc", vmu)]
        vxi = emit("vector", lambda e: e.tensor_mul(
            out=A("zgx")[:], in0=A("zgx")[:], in1=A("zmu")[:]), sem="v",
            waits=xi_w)
        wait("vector", "v", st["carry_v"])
        for c in range(NCH):
            emit("vector", (lambda e, c=c: e.tensor_tensor_scan(
                out=ch_sl("s16p", c), data0=ch_sl("za", c),
                data1=ch_sl("zgx", c), initial=A("carry")[:, c:c + 1],
                op0=ALU.mult, op1=ALU.add)),
                sem="v" if c == NCH - 1 else None,
                waits=[("v", vxi), ("sc", va)] if c == 0 else ())
        for c in range(NCH):
            emit("vector", (lambda e, c=c: e.tensor_copy(
                out=A("carry")[:, c:c + 1],
                in_=A("s16p")[:, c * T + T - 1:c * T + T])),
                sem="v" if c == NCH - 1 else None)
        st["carry_v"] = ctr["v"]
        st["gates_sc_prev"] = v_gates_sc
        st["zx32_free"] = [("v", v_xn), ("sc", vc2)]
        if debug and "s" in debug and own:
            dma("scalar", dbg_s[tile], A("s16p")[:], "iq",
                waits=[("v", st["carry_v"])])
        if debug and "xn" in debug and own:
            dma("scalar", dbg_xn[tile], A("xn16p")[:], "iq",
                waits=[("v", v_xn)])

    w1s_v = w1s.rearrange("(k p) h -> p k h", p=128)
    w3s_v = w3s.rearrange("(k p) h -> p k h", p=128)
    w2s_v = w2s.rearrange("(g p) d -> p g d", p=128)

    def emit_h128(t, m):
        p = m % 2
        w_ = [("v", st["hb_v"][(t, m)])]
        if (t, m - 1) in st["sc_hev"]:
            w_.append(("sc", st["sc_hev"][(t, m - 1)]))
        emit("tensor", (lambda e, p=p: e.matmul(
            ps["psC"].ap()[:], A("h128")[:], A(f"hb{p}")[:],
            start=True, stop=True)), sem="t", waits=w_)
        st["h128_t"][(t, m)] = ctr["t"]
        st["sc_hev"][(t, m)] = emit("scalar", (lambda e, m=m: e.activation(
            out=ch_sl("hd2", m), in_=ps["psC"].ap()[:],
            func=AF.Copy, scale=FWHT_SCALE)), sem="sc",
            waits=[("t", st["h128_t"][(t, m)])])
        st["hd_evac_sc"] = st["sc_hev"][(t, m)]

    def ffn(t):
        t0 = t * T
        first = (t == 0)
        for m in range(NH):
            p = m % 2
            w1sem = ["w1a", "w1b_"][p]
            w3sem = ["w3a", "w3b_"][p]
            prev_w1 = st["w1_consumed_t"].get(
                (t, m - 2), st["w1_consumed_t"].get((t - 1, NH - 2 + p), 0))
            w_waits = [("t", prev_w1)]
            if first and m == 0:
                w_waits.append(("pp13", v_pp13))
            w1b_v = A(f"w1b{p}").rearrange("p (k j) -> p k j", k=NCH)
            w3b_v = A(f"w3b{p}").rearrange("p (k j) -> p k j", k=NCH)
            vw1 = dma("sync", w1b_v, w1s_v[:, :, m * 128:(m + 1) * 128],
                      w1sem, waits=w_waits)
            vw3 = dma("sync", w3b_v, w3s_v[:, :, m * 128:(m + 1) * 128], w3sem)
            w_ = [(w1sem, vw1)]
            if (t, m - 1) in st["silu_sc"]:
                w_.append(("sc", st["silu_sc"][(t, m - 1)]))
            elif m == 0:
                w_.append(("sc", st["gates_sc_prev"]))
            for k in range(NCH):
                emit("tensor", (lambda e, k=k, p=p: e.matmul(
                    ps["psA"].ap()[:], A(f"w1b{p}")[:, k * 128:(k + 1) * 128],
                    ch_sl("xn16p", k), start=(k == 0), stop=(k == NCH - 1))),
                    sem="t" if k == NCH - 1 else None,
                    waits=w_ if k == 0 else ())
            st["ug_t"][(t, m)] = ctr["t"]
            if m >= 1:
                emit_h128(t, m - 1)
            w_ = [("t", st["ug_t"][(t, m)])]
            if (t, m - 2) in st["hb_v"]:
                w_.append(("v", st["hb_v"][(t, m - 2)]))
            st["silu_sc"][(t, m)] = emit("scalar", (lambda e, p=p: e.activation(
                out=A(f"sg{p}")[:], in_=ps["psA"].ap()[:], func=AF.Silu)),
                sem="sc", waits=w_)
            w_ = [(w3sem, vw3)]
            if (t, m - 1) in st["hb_v"]:
                w_.append(("v", st["hb_v"][(t, m - 1)]))
            for k in range(NCH):
                emit("tensor", (lambda e, k=k, p=p: e.matmul(
                    ps["psB"].ap()[:], A(f"w3b{p}")[:, k * 128:(k + 1) * 128],
                    ch_sl("xn16p", k), start=(k == 0), stop=(k == NCH - 1))),
                    sem="t" if k == NCH - 1 else None,
                    waits=w_ if k == 0 else ())
            st["ul_t"][(t, m)] = ctr["t"]
            st["w1_consumed_t"][(t, m)] = ctr["t"]
            w_ = [("t", st["ul_t"][(t, m)]), ("sc", st["silu_sc"][(t, m)])]
            if (t, m - 2) in st["h128_t"]:
                w_.append(("t", st["h128_t"][(t, m - 2)]))
            st["hb_v"][(t, m)] = emit("vector", (lambda e, p=p: e.tensor_mul(
                out=A(f"hb{p}")[:], in0=A(f"sg{p}")[:], in1=ps["psB"].ap()[:])),
                sem="v", waits=w_)
        emit_h128(t, NH - 1)
        HT = T // 2
        wait("vector", "sc", st["hd_evac_sc"])
        v_h64 = {}

        def hsl(buf, c, hh):
            return sb[buf].ap()[:, c * T + hh * HT:c * T + (hh + 1) * HT]

        for hh in range(2):
            for i in range(32):
                j = i + 32
                emit("vector", (lambda e, i=i, hh=hh: e.tensor_copy(
                    out=A("hb0")[:, hh * HT:(hh + 1) * HT],
                    in_=hsl("hd2", i, hh))), sem=None)
                emit("vector", (lambda e, i=i, j=j, hh=hh: e.tensor_add(
                    out=hsl("hd2", i, hh), in0=hsl("hd2", i, hh),
                    in1=hsl("hd2", j, hh))), sem=None)
                emit("vector", (lambda e, i=i, j=j, hh=hh: e.tensor_sub(
                    out=hsl("hd2", j, hh),
                    in0=A("hb0")[:, hh * HT:(hh + 1) * HT],
                    in1=hsl("hd2", j, hh))), sem=None)
            for stg in range(5):
                d = 1 << stg
                src = "hd2" if stg % 2 == 0 else "hd3"
                dst = "hd3" if stg % 2 == 0 else "hd2"
                for i in range(NH):
                    if (i >> stg) & 1:
                        continue
                    j = i + d
                    emit("vector", (lambda e, i=i, j=j, s=src, dd=dst, hh=hh:
                        e.tensor_add(out=hsl(dd, i, hh), in0=hsl(s, i, hh),
                                     in1=hsl(s, j, hh))), sem=None)
                    emit("vector", (lambda e, i=i, j=j, s=src, dd=dst, hh=hh:
                        e.tensor_sub(out=hsl(dd, j, hh), in0=hsl(s, i, hh),
                                     in1=hsl(s, j, hh))),
                        sem="v" if (stg == 4 and j == NH - 1) else None)
            v_h64[hh] = ctr["v"]
        for hh in range(2):
            for dm in range(NCH):
                p = dm % 2
                w2sem = ["w2a", "w2b_"][p]
                ob = ps["psE"] if p == 0 else ps["psF"]
                obsl = ob.ap()[:, hh * HT:(hh + 1) * HT]
                for g in range(8):
                    gp = g % 2
                    gw = []
                    if first and hh == 0 and dm == 0 and g == 0:
                        gw.append(("pp2", v_pp2))
                    prev = st["w2_consumed_t"].get(
                        (t, hh, dm, g - 2),
                        st["w2_consumed_t"].get((t, hh, dm - 1, 6 + gp), 0)
                        if dm >= 1 else
                        (st["w2_consumed_t"].get((t, hh - 1, NCH - 1, 6 + gp), 0)
                         if hh >= 1 else
                         st["w2_consumed_t"].get((t - 1, 1, NCH - 1, 6 + gp), 0)))
                    gw.append(("t", prev))
                    w2b_v = A(f"w2b{gp}").rearrange("p (i d) -> p i d", i=8)
                    vw2 = dma("sync", w2b_v,
                              w2s_v[:, g * 8:(g + 1) * 8, dm * 128:(dm + 1) * 128],
                              w2sem, waits=gw)
                    w_ = [(w2sem, vw2)]
                    if g == 0:
                        w_.append(("v", v_h64[hh]))
                        if (t, hh, dm - 2) in st["g16_sc"]:
                            w_.append(("sc", st["g16_sc"][(t, hh, dm - 2)]))
                        elif hh == 1 and dm <= 1:
                            w_.append(("sc", st["g16_sc"][(t, 0, NCH - 2 + dm)]))
                    for i in range(8):
                        mm = g * 8 + i
                        emit("tensor", (lambda e, i=i, gp=gp, mm=mm, obsl=obsl,
                            hh=hh: e.matmul(obsl,
                                A(f"w2b{gp}")[:, i * 128:(i + 1) * 128],
                                hsl("hd3", mm, hh),
                                start=(mm == 0), stop=(mm == NH - 1))),
                            sem="t" if i == 7 else None,
                            waits=w_ if i == 0 else ())
                    st["w2_consumed_t"][(t, hh, dm, g)] = ctr["t"]
                v_ops = ctr["t"]
                w_ = [("t", v_ops)]
                if (t, hh, dm - 2) in st["y_v"]:
                    w_.append(("v", st["y_v"][(t, hh, dm - 2)]))
                elif hh == 1 and dm <= 1:
                    w_.append(("v", st["y_v"][(t, 0, NCH - 2 + dm)]))
                gsl = A(f"g16{p}")[:, hh * HT:(hh + 1) * HT]
                st["g16_sc"][(t, hh, dm)] = emit("scalar", (lambda e, gsl=gsl,
                    obsl=obsl: e.activation(out=gsl, in_=obsl, func=AF.Silu)),
                    sem="sc", waits=w_)
                ysem = ["ya", "yb"][p]
                xrsem = ["xr0", "xr1"][p]
                xrbuf = ["mean32", "rstd32"][p]
                xrsl = A(xrbuf)[:, hh * HT:(hh + 1) * HT]
                ysl = A(f"y32{p}")[:, hh * HT:(hh + 1) * HT]
                vxr = dma("sync", xrsl,
                          xfm[dm * 128:(dm + 1) * 128,
                              t0 + hh * HT:t0 + (hh + 1) * HT], xrsem,
                          waits=[("v", st["y_v"][(t, hh, dm - 2)])]
                          if (t, hh, dm - 2) in st["y_v"] else ())
                ssl = sb["s16p"].ap()[:, dm * T + hh * HT:dm * T + (hh + 1) * HT]
                w_ = [("sc", st["g16_sc"][(t, hh, dm)]), (ysem, ctr[ysem])]
                emit("vector", (lambda e, ysl=ysl, gsl=gsl, ssl=ssl:
                    e.tensor_mul(out=ysl, in0=gsl, in1=ssl)), sem=None, waits=w_)
                st["y_v"][(t, hh, dm)] = emit("vector", (lambda e, ysl=ysl,
                    xrsl=xrsl: e.tensor_add(out=ysl, in0=ysl, in1=xrsl)),
                    sem="v", waits=[(xrsem, vxr)])
                gw = [("v", st["y_v"][(t, hh, dm)])]
                dma("gpsimd", y[dm * 128:(dm + 1) * 128,
                                t0 + hh * HT:t0 + (hh + 1) * HT],
                    ysl, ysem, waits=gw)
        st["dm_done_t"] = ctr["t"]

    bar = debug if isinstance(debug, (set, frozenset)) else set()
    for pt in range(4):
        phase1(pt)
        if "bp" in bar:
            barrier(f"pt{pt}", full="full" in bar)
    for t in range(NT):
        phase1(4 + t)
        if "bp" in bar:
            barrier(f"pt{4+t}", full="full" in bar)
        ffn(t)
        if "bf" in bar:
            barrier(f"ffn{t}", full="full" in bar)

    wait("gpsimd", "ya", ctr["ya"])
    wait("gpsimd", "yb", ctr["yb"])

    with contextlib.ExitStack() as es:
        sems = {s: es.enter_context(nc.semaphore(f"sem_{s}")) for s in SEM_NAMES}
        block = es.enter_context(nc.Block())

        def mk(eng_name):
            def body(eng):
                for item in ops[eng_name]:
                    if item[0] == "wait":
                        _, s, v = item
                        eng.wait_ge(sems[s], v)
                    else:
                        _, fn, s, amt = item
                        inst = fn(eng)
                        if s is not None:
                            inst.then_inc(sems[s], amt)
            return body

        block.sync(mk("sync"))
        block.tensor(mk("tensor"))
        block.vector(mk("vector"))
        block.scalar(mk("scalar"))
        block.gpsimd(mk("gpsimd"))

    return nc


_H128 = None


def _hadamard128():
    global _H128
    if _H128 is None:
        h = np.array([[1.0]], np.float32)
        while h.shape[0] < 128:
            h = np.block([[h, h], [h, -h]])
        _H128 = np.ascontiguousarray(h.astype(np.float16))
    return _H128


def kernel(L0_attn_x, L0_attn_a_param, L0_attn_ig_w, L0_attn_ig_b,
           L0_attn_ag_w, L0_attn_ag_b, L0_ffn_W1, L0_ffn_W2, L0_ffn_W3):
    x = np.asarray(L0_attn_x, np.float32)
    B, L, _ = x.shape
    nc = build_program()

    w1t = np.ascontiguousarray(np.asarray(L0_ffn_W1, np.float32).T)
    w3t = np.ascontiguousarray(np.asarray(L0_ffn_W3, np.float32).T)
    w2t = np.ascontiguousarray(np.asarray(L0_ffn_W2, np.float32).T)
    igw = np.ascontiguousarray(np.asarray(L0_attn_ig_w, np.float32))
    agw = np.ascontiguousarray(np.asarray(L0_attn_ag_w, np.float32))
    igb = np.ascontiguousarray(np.asarray(L0_attn_ig_b, np.float32).T)
    agb = np.ascontiguousarray(np.asarray(L0_attn_ag_b, np.float32).T)
    apar = np.ascontiguousarray(
        np.asarray(L0_attn_a_param, np.float32).reshape(16, 128).T)
    h128 = _hadamard128()
    ones = np.ones((128, 128), np.float16)
    ones32 = np.ones((128, 128), np.float32)
    zeros_part = np.zeros((D, LCORE), np.float32)

    in_maps = []
    for c in range(8):
        b, half = c // 2, c % 2
        sl = slice(half * LCORE, (half + 1) * LCORE)
        xfm = np.ascontiguousarray(x[b, sl, :].T)
        xp = (np.ascontiguousarray(x[b, 0:LCORE, :].T) if half == 1
              else zeros_part)
        cstv = np.zeros((128, 4), np.float32)
        cstv[:, 0] = 1.0
        cstv[:, 1] = EPS
        cstv[:, 2] = 1.0 if half == 0 else 0.0
        cstv[:, 3] = 0.0 if half == 0 else 1.0
        in_maps.append({
            "xfm": xfm, "xpart": xp, "w1t": w1t, "w3t": w3t, "w2t": w2t,
            "igw": igw, "agw": agw, "igb": igb, "agb": agb, "apar": apar,
            "h128": h128, "ones128": ones, "ones32d": ones32, "cst": cstv,
        })

    res = run_bass_kernel_spmd(nc, in_maps, core_ids=list(range(8)))

    out = np.empty((B, L, D), np.float32)
    for c in range(8):
        b, half = c // 2, c % 2
        sl = slice(half * LCORE, (half + 1) * LCORE)
        out[b, sl, :] = res.results[c]["y"].T
    return out


# version 03
# speedup vs baseline: 1.3553x; 1.3553x over previous
import contextlib
import numpy as np
import concourse.bass as bass
import concourse.mybir as mybir
from concourse.bass_utils import run_bass_kernel_spmd

F32 = mybir.dt.float32
F16 = mybir.dt.float16
AF = mybir.ActivationFunctionType
ALU = mybir.AluOpType

D = 2048
NCH = 16
T = 512
NT = 4
HD = 8192
NH = 64
LCORE = 2048
FWHT_SCALE = float(1.0 / np.sqrt(HD))
EPS = 1e-5

ENGINES = ("sync", "tensor", "vector", "scalar", "gpsimd")
SEM_NAMES = ["t", "v", "sc", "cast", "ypf", "pp13_0", "pp13_1", "pp13_2", "pp13_3", "pp2", "xr0", "xr1",
             "ya", "yb", "w1a", "w1b_", "w3a", "w3b_", "w2a", "w2b_",
             "xa", "xb", "iq"]


def build_program(debug=False):
    nc = bass.Bass("TRN2", target_bir_lowering=False)

    xfm = nc.dram_tensor("xfm", [D, LCORE], F32, kind="ExternalInput").ap()
    xpart = nc.dram_tensor("xpart", [D, LCORE], F32, kind="ExternalInput").ap()
    w1t = nc.dram_tensor("w1t", [D, HD], F32, kind="ExternalInput").ap()
    w3t = nc.dram_tensor("w3t", [D, HD], F32, kind="ExternalInput").ap()
    w2t = nc.dram_tensor("w2t", [HD, D], F32, kind="ExternalInput").ap()
    igw = nc.dram_tensor("igw", [16, 128, 128], F32, kind="ExternalInput").ap()
    agw = nc.dram_tensor("agw", [16, 128, 128], F32, kind="ExternalInput").ap()
    igb = nc.dram_tensor("igb", [128, 16], F32, kind="ExternalInput").ap()
    agb = nc.dram_tensor("agb", [128, 16], F32, kind="ExternalInput").ap()
    apar = nc.dram_tensor("apar", [128, 16], F32, kind="ExternalInput").ap()
    h128 = nc.dram_tensor("h128", [128, 128], F16, kind="ExternalInput").ap()
    ones128 = nc.dram_tensor("ones128", [128, 128], F16, kind="ExternalInput").ap()
    ones32d = nc.dram_tensor("ones32d", [128, 128], F32, kind="ExternalInput").ap()
    cst = nc.dram_tensor("cst", [128, 4], F32, kind="ExternalInput").ap()
    y = nc.dram_tensor("y", [D, LCORE], F32, kind="ExternalOutput").ap()
    if debug:
        dbg_s = nc.dram_tensor("dbg_s", [NT, 128, NCH * T], F16, kind="ExternalOutput").ap()
        dbg_y = nc.dram_tensor("dbg_y", [NT, NCH, 128, T], F32, kind="ExternalOutput").ap()
        dbg_xn = nc.dram_tensor("dbg_xn", [NT, 128, NCH * T], F16, kind="ExternalOutput").ap()
    w1s = nc.dram_tensor("w1s", [D, HD], F16, kind="Internal").ap()
    w3s = nc.dram_tensor("w3s", [D, HD], F16, kind="Internal").ap()
    w2s = nc.dram_tensor("w2s", [HD, D], F16, kind="Internal").ap()

    sb = {}

    def S(name, shape, dtype):
        sb[name] = nc.alloc_sbuf_tensor("sb_" + name, shape, dtype)

    S("cst", [128, 4], F32)
    S("h128", [128, 128], F16)
    S("ones", [128, 128], F16)
    S("ones32", [128, 128], F32)
    S("igw16", [128, NCH * 128], F16)
    S("agw16", [128, NCH * 128], F16)
    S("igb", [128, 16], F32)
    S("agb", [128, 16], F32)
    S("apar", [128, 16], F32)
    S("spm", [128, 16], F32)
    S("carry", [128, 16], F32)
    S("xn16p", [128, NCH * T], F16)
    S("s16p", [128, NCH * T], F16)
    S("mean32", [128, T], F32)
    S("rstd32", [128, T], F32)
    S("vv32", [128, T], F32)
    for p in range(2):
        S(f"w1b{p}", [128, NCH * 128], F16)
        S(f"w3b{p}", [128, NCH * 128], F16)
        S(f"w2b{p}", [128, 8 * 128], F16)
        S(f"sg{p}", [128, T], F16)
        S(f"hb{p}", [128, T], F16)
        S(f"g16{p}", [128, T], F16)
        S(f"y32{p}", [128, T], F32)
    o2 = nc.sbuf_base
    S("hd2", [128, NH * T], F16)
    o3 = nc.sbuf_base
    S("hd3", [128, NH * T], F16)
    sb["zx32"] = nc.alloc_sbuf_tensor_at("sb_zx32", [128, NCH * T], F32, offset=o2)
    sb["zx16"] = nc.alloc_sbuf_tensor_at("sb_zx16", [128, NCH * T], F16,
                                         offset=o2 + 32768)
    sb["zsq"] = nc.alloc_sbuf_tensor_at("sb_zsq", [128, NCH * T], F16,
                                        offset=o2 + 49152)
    sb["zgx"] = nc.alloc_sbuf_tensor_at("sb_zgx", [128, NCH * T], F16, offset=o3)
    sb["zga"] = nc.alloc_sbuf_tensor_at("sb_zga", [128, NCH * T], F16,
                                        offset=o3 + 16384)
    sb["za"] = nc.alloc_sbuf_tensor_at("sb_za", [128, NCH * T], F16,
                                       offset=o3 + 32768)
    sb["zmu"] = nc.alloc_sbuf_tensor_at("sb_zmu", [128, NCH * T], F16,
                                        offset=o3 + 49152)

    ps = {}
    for name in ("psA", "psB", "psC", "psD", "psE", "psF", "psG", "psH"):
        ps[name] = nc.alloc_psum_tensor("ps_" + name, [128, T], F32)

    ops = {e: [] for e in ENGINES}
    ctr = {s: 0 for s in SEM_NAMES}

    def inc(sem, amt):
        ctr[sem] += amt
        return ctr[sem]

    def wait(eng, sem, val):
        assert val <= ctr[sem], (eng, sem, val, ctr[sem])
        if val > 0:
            ops[eng].append(("wait", sem, val))

    def emit(eng, fn, sem=None, amt=1, waits=()):
        for (s, v) in waits:
            wait(eng, s, v)
        if sem is None:
            ops[eng].append(("op", fn, None, 0))
            return None
        v = inc(sem, amt)
        ops[eng].append(("op", fn, sem, amt))
        return v

    def dma(q, out_ap, in_ap, sem, waits=(), **kw):
        return emit(q, (lambda e, o=out_ap, i=in_ap, kw=kw:
                        e.dma_start(out=o, in_=i, **kw)), sem=sem, amt=16,
                    waits=waits)

    def barrier(tag, full=False):
        tv, vv, sv = ctr["t"], ctr["v"], ctr["sc"]
        engs = ("tensor", "vector", "scalar", "sync", "gpsimd") if full else                ("tensor", "vector", "scalar")
        sems = list(ctr.items()) if full else [("t", tv), ("v", vv), ("sc", sv)]
        for eng in engs:
            for s, v in sems:
                if v > 0:
                    ops[eng].append(("wait", s, v))

    def A(name):
        return sb[name].ap()

    def ch_sl(buf, c, width=T):
        return sb[buf].ap()[:, c * width:(c + 1) * width]

    dma("sync", A("cst")[:], cst, "iq")
    dma("sync", A("h128")[:], h128, "iq")
    dma("sync", A("ones")[:], ones128, "iq")
    dma("sync", A("ones32")[:], ones32d, "iq")
    dma("sync", A("igb")[:], igb, "iq")
    dma("sync", A("agb")[:], agb, "iq")
    dma("sync", A("apar")[:], apar, "iq")
    v_init_loads = ctr["iq"]
    igw_v = igw.rearrange("h i j -> i h j")
    agw_v = agw.rearrange("h i j -> i h j")
    igw16_v = A("igw16").rearrange("p (h j) -> p h j", h=16)
    agw16_v = A("agw16").rearrange("p (h j) -> p h j", h=16)
    dma("gpsimd", igw16_v, igw_v, "cast")
    dma("gpsimd", agw16_v, agw_v, "cast")
    v_cast = ctr["cast"]
    v_pp13s = []
    for grp in range(4):
        s_ = f"pp13_{grp}"
        for j in range(grp * 4, (grp + 1) * 4):
            dma("gpsimd", w1s[:, j * 512:(j + 1) * 512],
                w1t[:, j * 512:(j + 1) * 512], s_)
            dma("gpsimd", w3s[:, j * 512:(j + 1) * 512],
                w3t[:, j * 512:(j + 1) * 512], s_)
        v_pp13s.append(ctr[s_])
    for j in range(4):
        dma("gpsimd", w2s[:, j * 512:(j + 1) * 512],
            w2t[:, j * 512:(j + 1) * 512], "pp2")
    v_pp2 = ctr["pp2"]

    v0 = emit("scalar", lambda e: e.activation(out=A("spm")[:], in_=A("apar")[:],
                                               func=AF.Exp), sem="sc",
              waits=[("iq", v_init_loads)])
    v1 = emit("vector", lambda e: e.tensor_scalar_add(out=A("spm")[:],
                                                      in0=A("spm")[:],
                                                      scalar1=1.0), sem="v",
              waits=[("sc", v0)])
    v2 = emit("scalar", lambda e: e.activation(out=A("spm")[:], in_=A("spm")[:],
                                               func=AF.Ln), sem="sc",
              waits=[("v", v1)])
    v3 = emit("vector", lambda e: e.tensor_scalar_mul(out=A("spm")[:],
                                                      in0=A("spm")[:],
                                                      scalar1=-8.0), sem="v",
              waits=[("sc", v2)])
    v_carry = emit("vector", lambda e: e.memset(A("carry")[:], 0.0), sem="v")

    st = {
        "carry_v": v_carry,
        "dm_done_t": 0,
        "hd_evac_sc": 0,
        "gates_sc_prev": 0,
        "zx32_free": [],
        "silu_sc": {}, "hb_v": {}, "h128_t": {}, "sc_hev": {},
        "ug_t": {}, "ul_t": {}, "w1_consumed_t": {}, "w2_consumed_t": {},
        "g16_sc": {}, "y_v": {},
    }

    def phase1(pt):
        own = pt >= 4
        tile = pt % 4
        src = xfm if own else xpart
        xsem = ["xa", "xb"][pt % 2]
        src_v = src.rearrange("(c p) t -> p c t", p=128)[:, :, tile * T:(tile + 1) * T]
        dst_v = A("zx32").rearrange("p (c t) -> p c t", c=NCH)
        xw = [("t", st["dm_done_t"])] + st["zx32_free"]
        vx = dma("sync", dst_v, src_v, xsem, waits=xw)
        vc2 = emit("scalar", lambda e: e.activation(out=A("zsq")[:],
                                                    in_=A("zx32")[:],
                                                    func=AF.Square), sem="sc",
                   waits=[(xsem, vx)])
        vc1 = emit("vector", lambda e: e.tensor_copy(out=A("zx16")[:],
                                                     in_=A("zx32")[:]),
                   sem="v", waits=[(xsem, vx)])
        for c in range(NCH):
            emit("tensor", (lambda e, c=c: e.matmul(
                ps["psA"].ap()[:], A("ones")[:], ch_sl("zx16", c),
                start=(c == 0), stop=(c == NCH - 1))),
                sem="t" if c == NCH - 1 else None,
                waits=[("v", vc1), ("t", st["dm_done_t"])] if c == 0 else ())
        v_ss = ctr["t"]
        for c in range(NCH):
            emit("tensor", (lambda e, c=c: e.matmul(
                ps["psB"].ap()[:], A("ones")[:], ch_sl("zsq", c),
                start=(c == 0), stop=(c == NCH - 1))),
                sem="t" if c == NCH - 1 else None,
                waits=[("sc", vc2)] if c == 0 else ())
        v_sq = ctr["t"]
        emit("vector", lambda e: e.tensor_scalar_mul(
            out=A("mean32")[:], in0=ps["psA"].ap()[:], scalar1=1.0 / D),
            sem="v", waits=[("t", v_ss)])
        emit("vector", lambda e: e.tensor_scalar_mul(
            out=A("vv32")[:], in0=ps["psB"].ap()[:], scalar1=1.0 / D),
            sem="v", waits=[("t", v_sq)])
        emit("vector", lambda e: e.tensor_mul(
            out=A("rstd32")[:], in0=A("mean32")[:], in1=A("mean32")[:]), sem="v")
        vr = emit("vector", lambda e: e.tensor_sub(
            out=A("vv32")[:], in0=A("vv32")[:], in1=A("rstd32")[:]), sem="v")
        vsq = emit("scalar", lambda e: e.activation(
            out=A("vv32")[:], in_=A("vv32")[:], func=AF.Ln,
            bias=A("cst")[:, 1:2]), sem="sc", waits=[("v", vr)])
        vrc2 = emit("scalar", lambda e: e.activation(
            out=A("rstd32")[:], in_=A("vv32")[:], func=AF.Exp,
            scale=-0.5), sem="sc")
        bank_ig = ["psA", "psC"]
        bank_ag = ["psB", "psD"]
        ev_ig, ev_ag = {}, {}
        v_xn_c = {}
        for c in range(NCH):
            emit("vector", (lambda e, c=c: e.tensor_sub(
                out=ch_sl("zx32", c), in0=ch_sl("zx32", c), in1=A("mean32")[:])),
                sem=None, waits=[("sc", vrc2)] if c == 0 else ())
            v_xn_c[c] = emit("vector", (lambda e, c=c: e.tensor_mul(
                out=ch_sl("xn16p", c), in0=ch_sl("zx32", c), in1=A("rstd32")[:])),
                sem="v")
        v_xn = ctr["v"]
        for c in range(NCH):
            big, bag = bank_ig[c % 2], bank_ag[c % 2]
            w_ = [("v", v_xn_c[c])]
            if c == 0:
                w_.append(("cast", v_cast))
            if c >= 2:
                w_.append(("sc", ev_ig[c - 2]))
            vt1 = emit("tensor", (lambda e, c=c, b=big: e.matmul(
                ps[b].ap()[:], A("igw16")[:, c * 128:(c + 1) * 128],
                ch_sl("xn16p", c), start=True, stop=True)), sem="t", waits=w_)
            w2_ = [("sc", ev_ag[c - 2])] if c >= 2 else []
            vt2 = emit("tensor", (lambda e, c=c, b=bag: e.matmul(
                ps[b].ap()[:], A("agw16")[:, c * 128:(c + 1) * 128],
                ch_sl("xn16p", c), start=True, stop=True)), sem="t", waits=w2_)
            ev_ig[c] = emit("scalar", (lambda e, c=c, b=big: e.activation(
                out=ch_sl("zgx", c), in_=ps[b].ap()[:], func=AF.Sigmoid,
                bias=A("igb")[:, c:c + 1])), sem="sc", waits=[("t", vt1)])
            ev_ag[c] = emit("scalar", (lambda e, c=c, b=bag: e.activation(
                out=ch_sl("zga", c), in_=ps[b].ap()[:], func=AF.Sigmoid,
                bias=A("agb")[:, c:c + 1])), sem="sc", waits=[("t", vt2)])
        v_gates_sc = ctr["sc"]
        for c in range(NCH):
            emit("vector", (lambda e, c=c: e.tensor_scalar_mul(
                out=ch_sl("zga", c), in0=ch_sl("zga", c),
                scalar1=A("spm")[:, c:c + 1])),
                sem="v" if c == NCH - 1 else None,
                waits=[("sc", v_gates_sc), ("v", v3)] if c == 0 else ())
        v_loga = ctr["v"]
        va = emit("scalar", lambda e: e.activation(
            out=A("za")[:], in_=A("zga")[:], func=AF.Exp), sem="sc",
            waits=[("v", v_loga)])
        emit("scalar", lambda e: e.activation(
            out=A("zsq")[:], in_=A("zga")[:], func=AF.Exp, scale=2.0), sem="sc")
        vmu = emit("scalar", lambda e: e.activation(
            out=A("zmu")[:], in_=A("zsq")[:], func=AF.Sqrt, scale=-1.0,
            bias=A("cst")[:, 0:1]), sem="sc")
        mu3 = A("zmu").rearrange("p (c t) -> p c t", c=NCH)[:, :, 0:1]
        if pt == 0:
            v_rst = emit("vector", lambda e: e.memset(mu3, 1.0), sem="v",
                         waits=[("sc", vmu)])
        elif pt == 4:
            v_rst = emit("vector", lambda e: e.tensor_scalar(
                out=mu3, in0=mu3, scalar1=A("cst")[:, 3:4],
                scalar2=A("cst")[:, 2:3], op0=ALU.mult, op1=ALU.add), sem="v",
                waits=[("sc", vmu)])
        else:
            v_rst = None
        emit("vector", lambda e: e.tensor_mul(
            out=A("zgx")[:], in0=A("xn16p")[:], in1=A("zgx")[:]), sem=None,
            waits=[("sc", v_gates_sc)])
        xi_w = [("v", v_rst)] if v_rst is not None else [("sc", vmu)]
        vxi = emit("vector", lambda e: e.tensor_mul(
            out=A("zgx")[:], in0=A("zgx")[:], in1=A("zmu")[:]), sem="v",
            waits=xi_w)
        wait("vector", "v", st["carry_v"])
        for c in range(NCH):
            emit("vector", (lambda e, c=c: e.tensor_tensor_scan(
                out=ch_sl("s16p", c), data0=ch_sl("za", c),
                data1=ch_sl("zgx", c), initial=A("carry")[:, c:c + 1],
                op0=ALU.mult, op1=ALU.add)),
                sem="v" if c == NCH - 1 else None,
                waits=[("v", vxi), ("sc", va)] if c == 0 else ())
        for c in range(NCH):
            emit("vector", (lambda e, c=c: e.tensor_copy(
                out=A("carry")[:, c:c + 1],
                in_=A("s16p")[:, c * T + T - 1:c * T + T])),
                sem="v" if c == NCH - 1 else None)
        st["carry_v"] = ctr["v"]
        st["gates_sc_prev"] = v_gates_sc
        st["zx32_free"] = [("v", v_xn), ("sc", vc2)]
        if debug and "s" in debug and own:
            dma("scalar", dbg_s[tile], A("s16p")[:], "iq",
                waits=[("v", st["carry_v"])])
        if debug and "xn" in debug and own:
            dma("scalar", dbg_xn[tile], A("xn16p")[:], "iq",
                waits=[("v", v_xn)])

    w1s_v = w1s.rearrange("(k p) h -> p k h", p=128)
    w3s_v = w3s.rearrange("(k p) h -> p k h", p=128)
    w2s_v = w2s.rearrange("(g p) d -> p g d", p=128)

    def _h64_schedule():
        avail = {(-1, c): c for c in range(NH)}
        buckets = {m: [] for m in range(NH)}
        for stg in range(6):
            d = 1 << (5 - stg)
            for i in range(NH):
                if i & d:
                    continue
                j = i + d
                fm = max(avail[(stg - 1, i)], avail[(stg - 1, j)])
                avail[(stg, i)] = fm
                avail[(stg, j)] = fm
                buckets[fm].append((stg, i, j))
        return buckets

    H64_BUCKETS = _h64_schedule()

    def ffn(t):
        t0 = t * T
        first = (t == 0)
        for m in range(NH):
            p = m % 2
            w1sem = ["w1a", "w1b_"][p]
            w3sem = ["w3a", "w3b_"][p]
            prev_w1 = st["w1_consumed_t"].get(
                (t, m - 2), st["w1_consumed_t"].get((t - 1, NH - 2 + p), 0))
            w_waits = [("t", prev_w1)]
            if first and m % 16 == 0:
                w_waits.append((f"pp13_{m // 16}", v_pp13s[m // 16]))
            w1b_v = A(f"w1b{p}").rearrange("p (k j) -> p k j", k=NCH)
            w3b_v = A(f"w3b{p}").rearrange("p (k j) -> p k j", k=NCH)
            vw1 = dma("sync", w1b_v, w1s_v[:, :, m * 128:(m + 1) * 128],
                      w1sem, waits=w_waits)
            vw3 = dma("sync", w3b_v, w3s_v[:, :, m * 128:(m + 1) * 128], w3sem)
            w_ = [(w1sem, vw1)]
            if (t, m - 1) in st["silu_sc"]:
                w_.append(("sc", st["silu_sc"][(t, m - 1)]))
            elif m == 0:
                w_.append(("sc", st["gates_sc_prev"]))
            for k in range(NCH):
                emit("tensor", (lambda e, k=k, p=p: e.matmul(
                    ps["psA"].ap()[:], A(f"w1b{p}")[:, k * 128:(k + 1) * 128],
                    ch_sl("xn16p", k), start=(k == 0), stop=(k == NCH - 1))),
                    sem="t" if k == NCH - 1 else None,
                    waits=w_ if k == 0 else ())
            st["ug_t"][(t, m)] = ctr["t"]
            w_ = [("t", st["ug_t"][(t, m)])]
            if (t, m - 2) in st["hb_v"]:
                w_.append(("v", st["hb_v"][(t, m - 2)]))
            st["silu_sc"][(t, m)] = emit("scalar", (lambda e, p=p: e.activation(
                out=A(f"sg{p}")[:], in_=ps["psA"].ap()[:], func=AF.Silu)),
                sem="sc", waits=w_)
            w_ = [(w3sem, vw3)]
            if (t, m - 1) in st["hb_v"]:
                w_.append(("v", st["hb_v"][(t, m - 1)]))
            for k in range(NCH):
                emit("tensor", (lambda e, k=k, p=p: e.matmul(
                    ps["psB"].ap()[:], A(f"w3b{p}")[:, k * 128:(k + 1) * 128],
                    ch_sl("xn16p", k), start=(k == 0), stop=(k == NCH - 1))),
                    sem="t" if k == NCH - 1 else None,
                    waits=w_ if k == 0 else ())
            st["ul_t"][(t, m)] = ctr["t"]
            st["w1_consumed_t"][(t, m)] = ctr["t"]
            w_ = [("t", st["ul_t"][(t, m)]), ("sc", st["silu_sc"][(t, m)])]
            st["hb_v"][(t, m)] = emit("vector", (lambda e, p=p, m=m:
                e.tensor_mul(out=ch_sl("hd2", m), in0=A(f"sg{p}")[:],
                             in1=ps["psB"].ap()[:])), sem="v", waits=w_)
            for (stg, i, j) in H64_BUCKETS[m]:
                src = "hd2" if stg % 2 == 0 else "hd3"
                dst = "hd3" if stg % 2 == 0 else "hd2"
                emit("vector", (lambda e, i=i, j=j, s=src, dd=dst: e.tensor_add(
                    out=ch_sl(dd, i), in0=ch_sl(s, i), in1=ch_sl(s, j))),
                    sem=None)
                emit("vector", (lambda e, i=i, j=j, s=src, dd=dst: e.tensor_sub(
                    out=ch_sl(dd, j), in0=ch_sl(s, i), in1=ch_sl(s, j))),
                    sem="v" if (stg, j) == (5, NH - 1) else None)
        v_h64 = ctr["v"]
        hp = {}
        h128_banks = ["psC", "psD", "psG", "psH"]
        for m in range(NH):
            bank = ps[h128_banks[m % 4]]
            w_ = []
            if m == 0:
                w_.append(("v", v_h64))
            if (t, m - 4) in st["sc_hev"]:
                w_.append(("sc", st["sc_hev"][(t, m - 4)]))
            emit("tensor", (lambda e, b=bank, m=m: e.matmul(
                b.ap()[:], A("h128")[:], ch_sl("hd2", m),
                start=True, stop=True)), sem="t", waits=w_)
            hp[m] = ctr["t"]
            st["sc_hev"][(t, m)] = emit("scalar", (lambda e, b=bank, m=m:
                e.activation(out=ch_sl("hd3", m), in_=b.ap()[:],
                             func=AF.Copy, scale=FWHT_SCALE)), sem="sc",
                waits=[("t", hp[m])])
        st["hd_evac_sc"] = st["sc_hev"][(t, NH - 1)]
        v_hp_all = st["hd_evac_sc"]
        for dm in range(NCH):
            p = dm % 2
            w2sem = ["w2a", "w2b_"][p]
            ob = ps["psE"] if p == 0 else ps["psF"]
            for g in range(8):
                gp = g % 2
                gw = []
                if first and dm == 0 and g == 0:
                    gw.append(("pp2", v_pp2))
                prev = st["w2_consumed_t"].get(
                    (t, dm, g - 2),
                    st["w2_consumed_t"].get((t, dm - 1, 6 + gp), 0) if dm >= 1
                    else st["w2_consumed_t"].get((t - 1, NCH - 1, 6 + gp), 0))
                gw.append(("t", prev))
                w2b_v = A(f"w2b{gp}").rearrange("p (i d) -> p i d", i=8)
                vw2 = dma("sync", w2b_v,
                          w2s_v[:, g * 8:(g + 1) * 8, dm * 128:(dm + 1) * 128],
                          w2sem, waits=gw)
                w_ = [(w2sem, vw2)]
                if g == 0:
                    w_.append(("sc", v_hp_all))
                    if (t, dm - 2) in st["g16_sc"]:
                        w_.append(("sc", st["g16_sc"][(t, dm - 2)]))
                for i in range(8):
                    mm = g * 8 + i
                    emit("tensor", (lambda e, i=i, gp=gp, mm=mm, ob=ob: e.matmul(
                        ob.ap()[:], A(f"w2b{gp}")[:, i * 128:(i + 1) * 128],
                        ch_sl("hd3", mm), start=(mm == 0), stop=(mm == NH - 1))),
                        sem="t" if i == 7 else None,
                        waits=w_ if i == 0 else ())
                st["w2_consumed_t"][(t, dm, g)] = ctr["t"]
            v_ops = ctr["t"]
            w_ = [("t", v_ops)]
            if (t, dm - 2) in st["y_v"]:
                w_.append(("v", st["y_v"][(t, dm - 2)]))
            st["g16_sc"][(t, dm)] = emit("scalar", (lambda e, p=p, ob=ob:
                e.activation(out=A(f"g16{p}")[:], in_=ob.ap()[:], func=AF.Silu)),
                sem="sc", waits=w_)
            ysem = ["ya", "yb"][p]
            xrsem = ["xr0", "xr1"][p]
            xrbuf = ["mean32", "rstd32"][p]
            vxr = dma("sync", A(xrbuf)[:],
                      xfm[dm * 128:(dm + 1) * 128, t0:t0 + T], xrsem,
                      waits=[("v", st["y_v"][(t, dm - 2)])]
                      if (t, dm - 2) in st["y_v"] else ())
            w_ = [("sc", st["g16_sc"][(t, dm)]), (ysem, ctr[ysem])]
            emit("vector", (lambda e, p=p, dm=dm:
                e.tensor_mul(out=A(f"y32{p}")[:], in0=A(f"g16{p}")[:],
                             in1=ch_sl("s16p", dm))), sem=None, waits=w_)
            st["y_v"][(t, dm)] = emit("vector", (lambda e, p=p, b=xrbuf:
                e.tensor_add(out=A(f"y32{p}")[:], in0=A(f"y32{p}")[:],
                             in1=A(b)[:])), sem="v", waits=[(xrsem, vxr)])
            gw = [("v", st["y_v"][(t, dm)])]
            dma("gpsimd", y[dm * 128:(dm + 1) * 128, t0:t0 + T],
                A(f"y32{p}")[:], ysem, waits=gw)
        st["dm_done_t"] = ctr["t"]

    bar = debug if isinstance(debug, (set, frozenset)) else set()
    for pt in range(4):
        phase1(pt)
        if "bp" in bar:
            barrier(f"pt{pt}", full="full" in bar)
    for t in range(NT):
        phase1(4 + t)
        if "bp" in bar:
            barrier(f"pt{4+t}", full="full" in bar)
        ffn(t)
        if "bf" in bar:
            barrier(f"ffn{t}", full="full" in bar)

    wait("gpsimd", "ya", ctr["ya"])
    wait("gpsimd", "yb", ctr["yb"])

    with contextlib.ExitStack() as es:
        sems = {s: es.enter_context(nc.semaphore(f"sem_{s}")) for s in SEM_NAMES}
        block = es.enter_context(nc.Block())

        def mk(eng_name):
            def body(eng):
                for item in ops[eng_name]:
                    if item[0] == "wait":
                        _, s, v = item
                        eng.wait_ge(sems[s], v)
                    else:
                        _, fn, s, amt = item
                        inst = fn(eng)
                        if s is not None:
                            inst.then_inc(sems[s], amt)
            return body

        block.sync(mk("sync"))
        block.tensor(mk("tensor"))
        block.vector(mk("vector"))
        block.scalar(mk("scalar"))
        block.gpsimd(mk("gpsimd"))

    return nc


_H128 = None


def _hadamard128():
    global _H128
    if _H128 is None:
        h = np.array([[1.0]], np.float32)
        while h.shape[0] < 128:
            h = np.block([[h, h], [h, -h]])
        _H128 = np.ascontiguousarray(h.astype(np.float16))
    return _H128


def kernel(L0_attn_x, L0_attn_a_param, L0_attn_ig_w, L0_attn_ig_b,
           L0_attn_ag_w, L0_attn_ag_b, L0_ffn_W1, L0_ffn_W2, L0_ffn_W3):
    x = np.asarray(L0_attn_x, np.float32)
    B, L, _ = x.shape
    nc = build_program()

    w1t = np.ascontiguousarray(np.asarray(L0_ffn_W1, np.float32).T)
    w3t = np.ascontiguousarray(np.asarray(L0_ffn_W3, np.float32).T)
    w2t = np.ascontiguousarray(np.asarray(L0_ffn_W2, np.float32).T)
    igw = np.ascontiguousarray(np.asarray(L0_attn_ig_w, np.float32))
    agw = np.ascontiguousarray(np.asarray(L0_attn_ag_w, np.float32))
    igb = np.ascontiguousarray(np.asarray(L0_attn_ig_b, np.float32).T)
    agb = np.ascontiguousarray(np.asarray(L0_attn_ag_b, np.float32).T)
    apar = np.ascontiguousarray(
        np.asarray(L0_attn_a_param, np.float32).reshape(16, 128).T)
    h128 = _hadamard128()
    ones = np.ones((128, 128), np.float16)
    ones32 = np.ones((128, 128), np.float32)
    zeros_part = np.zeros((D, LCORE), np.float32)

    in_maps = []
    for c in range(8):
        b, half = c // 2, c % 2
        sl = slice(half * LCORE, (half + 1) * LCORE)
        xfm = np.ascontiguousarray(x[b, sl, :].T)
        xp = (np.ascontiguousarray(x[b, 0:LCORE, :].T) if half == 1
              else zeros_part)
        cstv = np.zeros((128, 4), np.float32)
        cstv[:, 0] = 1.0
        cstv[:, 1] = EPS
        cstv[:, 2] = 1.0 if half == 0 else 0.0
        cstv[:, 3] = 0.0 if half == 0 else 1.0
        in_maps.append({
            "xfm": xfm, "xpart": xp, "w1t": w1t, "w3t": w3t, "w2t": w2t,
            "igw": igw, "agw": agw, "igb": igb, "agb": agb, "apar": apar,
            "h128": h128, "ones128": ones, "ones32d": ones32, "cst": cstv,
        })

    res = run_bass_kernel_spmd(nc, in_maps, core_ids=list(range(8)))

    out = np.empty((B, L, D), np.float32)
    for c in range(8):
        b, half = c // 2, c % 2
        sl = slice(half * LCORE, (half + 1) * LCORE)
        out[b, sl, :] = res.results[c]["y"].T
    return out
